# version 10
# baseline (speedup 1.0000x reference)
import numpy as np
from contextlib import ExitStack
import concourse.bass as bass
import concourse.mybir as mybir
from concourse.bass_utils import run_bass_kernel_spmd

F32 = mybir.dt.float32
BF16 = mybir.dt.bfloat16
AF = mybir.ActivationFunctionType
ALU = mybir.AluOpType

P = 128
D = 1024
DC = 8
MD = 3072
MC = 24
FD = 4096
FC = 32
L = 2048
NB = 4
NCORES = 8
TT = 1024
T2 = 512
NPASS = NB * (L // TT)
EPS_LN = 1e-5
EPS_RMS1 = 1e-6
EPS_RMS2 = 1e-6

WRCOLS = 11264


class Ctr:

    def __init__(self, handle, name):
        self.h = handle
        self.name = name
        self.n = 0

    def inc(self, amount=1):
        self.n += amount
        return self.n


class Emitter:
    def __init__(self, nc):
        self.nc = nc
        self.q = {"tensor": [], "scalar": [], "vector": [], "gpsimd": [], "sync": []}
        self.waited = {e: {} for e in self.q}

    def wait(self, eng, ctr, val=None):
        v = ctr.n if val is None else val
        if v <= 0:
            return
        w = self.waited[eng]
        if w.get(ctr.name, -1) >= v:
            return
        w[ctr.name] = v
        h = ctr.h
        self.q[eng].append(lambda e, h=h, v=v: e.wait_ge(h, v))

    def op(self, eng, fn, inc=None, amount=1):
        if inc is None:
            self.q[eng].append(fn)
            return None
        val = inc.inc(amount)
        h = inc.h

        def run(e, fn=fn, h=h, amount=amount):
            fn(e).then_inc(h, amount)

        self.q[eng].append(run)
        return val


class Ring:

    def __init__(self, em, wr, s_wq, s_pe):
        self.em = em
        self.wr = wr
        self.s_wq = s_wq
        self.s_pe = s_pe
        self.pos = 0
        self.active = []

    def load(self, src_ap, kc, gran=1):
        width = gran * P
        need = kc * width
        if self.pos + need > WRCOLS:
            self.pos = 0
        start, end = self.pos, self.pos + need
        self.pos = end
        overl = [r for r in self.active if r[0] < end and r[1] > start]
        for r in overl:
            assert r[2] is not None, "ring reuse before consume registered"
            self.em.wait("gpsimd", self.s_pe, r[2])
            self.active.remove(r)
        dst = self.wr[:, start:end].rearrange("p (kc o) -> p kc o", kc=kc)
        loaded = self.em.op(
            "gpsimd",
            lambda e, dst=dst, src=src_ap: e.dma_start(out=dst, in_=src),
            inc=self.s_wq, amount=16)
        rec = [start, end, None, loaded]
        self.active.append(rec)
        wr = self.wr

        def slab(oc, k, start=start, width=width):
            base = start + k * width + oc * P
            return wr[:, base: base + P]

        return slab, loaded, rec


DEBUG_DUMPS = False


def build_program():
    nc = bass.Bass("TRN2", target_bir_lowering=False)

    x_h = nc.dram_tensor("x", [NB, D, L], F32, kind="ExternalInput").ap()
    y_h = nc.dram_tensor("y", [NB, D, L], F32, kind="ExternalOutput").ap()

    wts = {}
    for name, k, o in [
        ("wt_fc", D, D), ("wt_r", D, D), ("wt_i", D, D), ("wt_gate", D, D),
        ("wt_mg", D, MD), ("wt_mf", D, MD), ("wt_out", MD, D),
        ("wt_w1", D, FD), ("wt_w2", D, FD), ("wt_h1", FD, FD), ("wt_h2", FD, FD),
        ("wt_w3", FD, D),
    ]:
        wts[name] = nc.dram_tensor(name, [k, o], F32, kind="ExternalInput").ap()

    vecs = {}
    for name, c in [
        ("b_fc", DC), ("b_r", DC), ("b_i", DC), ("b_gate", DC),
        ("b_mg", MC), ("b_mf", MC), ("b_out", DC),
        ("nw", DC), ("n2w", DC), ("alv", DC),
        ("s_w1", FC), ("s_w2", FC), ("s_h1", FC), ("s_h2", FC), ("s_w3", DC),
    ]:
        vecs[name] = nc.dram_tensor(name, [P, c], F32, kind="ExternalInput").ap()

    es = ExitStack()
    sb = lambda name, cols, dt=F32: es.enter_context(nc.sbuf_tensor(name, [P, cols], dt))

    X = sb("X", DC * TT)
    H = sb("H", DC * TT, BF16)
    A = sb("A", DC * TT)
    S = sb("S", DC * TT)
    SCR1 = sb("SCR1", 4 * TT)
    SCR2 = sb("SCR2", 4 * TT)
    SCR3 = sb("SCR3", 4 * TT)
    MT = sb("MT", TT)
    RST = sb("RST", TT)
    CC = sb("CC", TT)
    DS = sb("DS", 4 * T2)
    WR = sb("WR", WRCOLS, BF16)
    ONESB = sb("ONESB", P, BF16)
    COEF = sb("COEF", DC)
    CARRY = sb("CARRY", DC)
    ALV = sb("ALV", DC)
    EPS5 = sb("EPS5", 1)
    EPS11 = sb("EPS11", 1)
    EPS6T = sb("EPS6T", 1)
    ONE1 = sb("ONE1", 1)
    VB = {}
    for name, c in [("b_fc", DC), ("b_r", DC), ("b_i", DC), ("b_gate", DC),
                    ("b_mg", MC), ("b_mf", MC), ("b_out", DC), ("nw", DC),
                    ("n2w", DC), ("s_w1", FC), ("s_w2", FC), ("s_h1", FC),
                    ("s_h2", FC), ("s_w3", DC)]:
        VB[name] = sb("v_" + name, c)

    PS = [es.enter_context(nc.psum_tensor(f"ps{i}", [P, T2], F32)) for i in range(8)]

    s_pe = Ctr(es.enter_context(nc.semaphore("s_pe")), "s_pe")
    s_act = Ctr(es.enter_context(nc.semaphore("s_act")), "s_act")
    s_dve = Ctr(es.enter_context(nc.semaphore("s_dve")), "s_dve")
    s_wq = Ctr(es.enter_context(nc.semaphore("s_wq")), "s_wq")
    s_x = Ctr(es.enter_context(nc.semaphore("s_x")), "s_x")
    s_y = Ctr(es.enter_context(nc.semaphore("s_y")), "s_y")

    em = Emitter(nc)
    ring = Ring(em, WR, s_wq, s_pe)

    def gload(name, og, gran, kc):
        src = wts[name][:, og * gran * P:(og + 1) * gran * P].rearrange(
            "(kc kp) o -> kp kc o", kp=P)
        return ring.load(src, kc, gran)

    dbg = {}
    s_dbg = Ctr(es.enter_context(nc.semaphore("s_dbg")), "s_dbg") if DEBUG_DUMPS else None
    if DEBUG_DUMPS:
        for name, cols, dt in [("dbg_h", DC * TT, BF16), ("dbg_a", DC * TT, F32),
                               ("dbg_s", DC * TT, F32), ("dbg_out", DC * TT, F32),
                               ("dbg_x1", DC * TT, F32), ("dbg_hs2", DC * TT, F32),
                               ("dbg_stats", 4 * TT, F32)]:
            dbg[name] = nc.dram_tensor(name, [P, cols], dt, kind="ExternalOutput").ap()

    def dump(name, src_ap, wait_list):
        if not DEBUG_DUMPS:
            return None
        for ctr, cnt in wait_list:
            em.wait("sync", ctr, cnt)
        return em.op("sync", lambda e, d=dbg[name], s=src_ap: e.dma_start(out=d, in_=s),
                     inc=s_dbg, amount=16)

    bank_mm = [0] * 8
    bank_ev = [None] * 8

    def Hc(k, tau):
        return H[:, k * TT + tau * T2: k * TT + (tau + 1) * T2]

    def Xc(k, tau):
        return X[:, k * TT + tau * T2: k * TT + (tau + 1) * T2]

    def Ac(k, tau):
        return A[:, k * TT + tau * T2: k * TT + (tau + 1) * T2]

    def Sc(k, tau):
        return S[:, k * TT + tau * T2: k * TT + (tau + 1) * T2]

    S_bf = S[:, :].bitcast(BF16)
    A_bf = A[:, :].bitcast(BF16)
    SCR1_bf = SCR1[:, :].bitcast(BF16)
    SCR2_bf = SCR2[:, :].bitcast(BF16)
    SCR3_bf = SCR3[:, :].bitcast(BF16)
    H_bf = H

    def P_tile(j, tau):
        idx = j * 2 + tau
        if idx < 32:
            return S_bf[:, idx * T2:(idx + 1) * T2]
        return SCR1_bf[:, (idx - 32) * T2:(idx - 31) * T2]

    def T_tile(j, tau):
        idx = j * 2 + tau
        if idx < 32:
            return A_bf[:, idx * T2:(idx + 1) * T2]
        return S_bf[:, (idx - 32) * T2:(idx - 31) * T2]

    def V_tile(j, tau):
        idx = j * 2 + tau
        if idx < 16:
            return SCR1_bf[:, idx * T2:(idx + 1) * T2]
        if idx < 32:
            return SCR2_bf[:, (idx - 16) * T2:(idx - 15) * T2]
        if idx < 48:
            return SCR3_bf[:, (idx - 32) * T2:(idx - 31) * T2]
        return H_bf[:, (idx - 48) * T2:(idx - 47) * T2]

    def DSs(i):
        return DS[:, i * T2:(i + 1) * T2]

    def cast_tile(buf_bf, k):
        return buf_bf[:, k * T2:(k + 1) * T2]

    def mm_group(bank, lhsT_fn, rhs_list, wait_wq=None, wait_inputs=()):
        if bank_ev[bank] is not None:
            ctr, cnt = bank_ev[bank]
            em.wait("tensor", ctr, cnt)
        if wait_wq is not None:
            em.wait("tensor", s_wq, wait_wq)
        for ctr, cnt in wait_inputs:
            em.wait("tensor", ctr, cnt)
        nk = len(rhs_list)
        for k in range(nk):
            lhsT = lhsT_fn(k)
            rhs = rhs_list[k]
            if k == nk - 1:
                cnt = em.op("tensor",
                            lambda e, o=PS[bank][:, :], l=lhsT, r=rhs, st=(k == 0):
                            e.matmul(o, l, r, start=st, stop=True),
                            inc=s_pe)
            else:
                em.op("tensor",
                      lambda e, o=PS[bank][:, :], l=lhsT, r=rhs, st=(k == 0):
                      e.matmul(o, l, r, start=st, stop=False))
        bank_mm[bank] = cnt
        return cnt

    def evac_wait(eng, bank):
        em.wait(eng, s_pe, bank_mm[bank])

    for name in VB:
        em.op("sync", lambda e, d=VB[name][:, :], s=vecs[name]: e.dma_start(out=d, in_=s),
              inc=s_x, amount=16)
    em.op("sync", lambda e: e.dma_start(out=ALV[:, :], in_=vecs["alv"]), inc=s_x, amount=16)
    vecs_loaded = s_x.n

    em.op("vector", lambda e: e.memset(ONESB[:, :], 1.0))
    em.op("vector", lambda e: e.memset(EPS5[:, :], EPS_LN))
    em.op("vector", lambda e: e.memset(EPS11[:, :], EPS_LN * EPS_RMS1))
    em.op("vector", lambda e: e.memset(EPS6T[:, :], EPS_RMS2))
    cfg_dve = em.op("vector", lambda e: e.memset(ONE1[:, :], 1.0), inc=s_dve)

    em.wait("scalar", s_x, vecs_loaded)
    em.wait("scalar", s_dve, cfg_dve)
    c = em.op("scalar", lambda e: e.activation(DS[:, 0:DC], ALV[:, :], AF.Abs), inc=s_act)
    em.wait("scalar", s_act, c)
    c = em.op("scalar", lambda e: e.activation(DS[:, DC:2 * DC], DS[:, 0:DC], AF.Exp,
                                               scale=-1.0), inc=s_act)
    em.wait("scalar", s_act, c)
    c_sp = em.op("scalar", lambda e: e.activation(DS[:, 0:DC], DS[:, DC:2 * DC], AF.Ln,
                                                  bias=ONE1[:, :]), inc=s_act)
    em.wait("vector", s_x, vecs_loaded)
    c = em.op("vector", lambda e: e.tensor_scalar(DS[:, 3 * DC:4 * DC], ALV[:, :], 0.0, None,
                                                  ALU.max), inc=s_dve)
    em.wait("vector", s_dve, c)
    em.wait("vector", s_act, c_sp)
    c = em.op("vector", lambda e: e.tensor_tensor(DS[:, 2 * DC:3 * DC], DS[:, 0:DC],
                                                  DS[:, 3 * DC:4 * DC], ALU.add), inc=s_dve)
    em.wait("vector", s_dve, c)
    coef_ready = em.op("vector", lambda e: e.tensor_scalar(COEF[:, :], DS[:, 2 * DC:3 * DC],
                                                           -8.0, None, ALU.mult), inc=s_dve)

    for p in range(NPASS):
        b, hh = divmod(p, L // TT)

        for o in range(DC):
            if p > 0:
                em.wait("sync", s_y, 16 * ((p - 1) * DC + o + 1))
            src = x_h[b, o * P:(o + 1) * P, hh * TT:(hh + 1) * TT]
            em.op("sync", lambda e, d=X[:, o * TT:(o + 1) * TT], s=src:
                  e.dma_start(out=d, in_=s), inc=s_x, amount=16)
        xload = s_x.n

        em.wait("scalar", s_dve, s_dve.n)
        em.wait("vector", s_act, s_act.n)
        em.wait("scalar", s_pe, s_pe.n)
        em.wait("vector", s_pe, s_pe.n)

        X3 = X[:, :].rearrange("p (kc t) -> p kc t", kc=DC)
        SCR1c = SCR1_bf[:, 0:DC * T2].rearrange("p (kc t) -> p kc t", kc=DC)
        SCR2c = SCR2_bf[:, 0:DC * T2].rearrange("p (kc t) -> p kc t", kc=DC)
        h_ready = [None, None]
        prev_stats = 0
        for tau in range(2):
            em.wait("scalar", s_x, xload)
            em.wait("scalar", s_pe, prev_stats)
            em.op("scalar", lambda e, o=SCR1c, i=X3[:, :, tau * T2:(tau + 1) * T2]:
                  e.activation(o, i, AF.Copy))
            casts = em.op("scalar", lambda e, o=SCR2c, i=X3[:, :, tau * T2:(tau + 1) * T2]:
                          e.activation(o, i, AF.Square), inc=s_act)
            em.wait("tensor", s_dve, cfg_dve)
            gS = mm_group(4, lambda k: ONESB[:, :],
                          [cast_tile(SCR1_bf, k) for k in range(DC)],
                          wait_inputs=[(s_act, casts)])
            gQ = mm_group(5, lambda k: ONESB[:, :],
                          [cast_tile(SCR2_bf, k) for k in range(DC)])
            prev_stats = max(gS, gQ)
            evac_wait("vector", 4)
            em.op("vector", lambda e, o=MT[:, tau * T2:(tau + 1) * T2], i=PS[4][:, :]:
                  e.tensor_scalar(o, i, 1.0 / D, None, ALU.mult))
            em.op("vector", lambda e, o=DSs(0), m=MT[:, tau * T2:(tau + 1) * T2]:
                  e.tensor_tensor(o, m, m, ALU.mult))
            evac_wait("vector", 5)
            var_c = em.op("vector", lambda e, o=DSs(1), i=PS[5][:, :], m=DSs(0):
                          e.scalar_tensor_tensor(o, i, 1.0 / D, m, ALU.mult, ALU.subtract),
                          inc=s_dve)
            bank_ev[4] = (s_dve, var_c)
            bank_ev[5] = (s_dve, var_c)
            em.wait("scalar", s_dve, var_c)
            em.op("scalar", lambda e, o=DSs(2), i=DSs(1):
                  e.activation(o, i, AF.Ln, bias=EPS5[:, :]))
            em.op("scalar", lambda e, o=RST[:, tau * T2:(tau + 1) * T2], i=DSs(2):
                  e.activation(o, i, AF.Exp, scale=-0.5))
            em.op("scalar", lambda e, o=DSs(2), i=DSs(1):
                  e.activation(o, i, AF.Ln, scale=1.0 + EPS_RMS1, bias=EPS11[:, :]))
            c1_c = em.op("scalar", lambda e, o=CC[:, tau * T2:(tau + 1) * T2], i=DSs(2):
                         e.activation(o, i, AF.Exp, scale=-0.5), inc=s_act)
            em.wait("vector", s_act, c1_c)
            for k in range(DC):
                em.op("vector", lambda e, o=DSs(3), xk=Xc(k, tau), m=MT[:, tau * T2:(tau + 1) * T2]:
                      e.tensor_tensor(o, xk, m, ALU.subtract))
                inc = s_dve if k == DC - 1 else None
                c = em.op("vector", lambda e, o=Hc(k, tau), d=DSs(3),
                          nwk=VB["nw"][:, k:k + 1], c1=CC[:, tau * T2:(tau + 1) * T2]:
                          e.scalar_tensor_tensor(o, d, nwk, c1, ALU.mult, ALU.mult), inc=inc)
            h_ready[tau] = c

        if p == 0 and DEBUG_DUMPS:
            c = dump("dbg_h", H[:, :], [(s_dve, h_ready[1])])
            em.op("sync", lambda e, d=dbg["dbg_stats"][:, 0:TT], s=MT[:, :]:
                  e.dma_start(out=d, in_=s), inc=s_dbg, amount=16)
            em.op("sync", lambda e, d=dbg["dbg_stats"][:, TT:2 * TT], s=RST[:, :]:
                  e.dma_start(out=d, in_=s), inc=s_dbg, amount=16)
            em.op("sync", lambda e, d=dbg["dbg_stats"][:, 2 * TT:3 * TT], s=CC[:, :]:
                  e.dma_start(out=d, in_=s), inc=s_dbg, amount=16)
            em.wait("vector", s_dbg, s_dbg.n)
            em.wait("scalar", s_dbg, s_dbg.n)

        a_counts = {}
        for tau in range(2):
            em.wait("scalar", s_dve, s_dve.n)
            sig_c = [None] * DC
            i_c = [None] * DC
            for og in range(DC // 4):
              slab_r, ld_r, rec_r = gload("wt_r", og, 4, DC)
              slab_i, ld_i, rec_i = gload("wt_i", og, 4, DC)
              for oc in range(4):
                o = og * 4 + oc
                bk_r = o % 2
                bk_i = 2 + o % 2
                g_r = mm_group(bk_r, (lambda k, s=slab_r, oc=oc: s(oc, k)),
                               [Hc(k, tau) for k in range(DC)],
                               wait_wq=ld_r, wait_inputs=[(s_dve, h_ready[tau])])
                rec_r[2] = g_r
                g_i = mm_group(bk_i, (lambda k, s=slab_i, oc=oc: s(oc, k)),
                               [Hc(k, tau) for k in range(DC)], wait_wq=ld_i)
                rec_i[2] = g_i
                em.wait("scalar", s_pe, max(gS, gQ))
                evac_wait("scalar", bk_r)
                r_cnt = em.op("scalar", lambda e, o_=SCR1[:, o * T2:(o + 1) * T2],
                              i=PS[bk_r][:, :], bv=VB["b_r"][:, o:o + 1]:
                              e.activation(o_, i, AF.Sigmoid, bias=bv), inc=s_act)
                bank_ev[bk_r] = (s_act, r_cnt)
                sig_c[o] = r_cnt
                evac_wait("scalar", bk_i)
                ic = em.op("scalar", lambda e, o_=SCR2[:, o * T2:(o + 1) * T2],
                           i=PS[bk_i][:, :], bv=VB["b_i"][:, o:o + 1]:
                           e.activation(o_, i, AF.Sigmoid, bias=bv), inc=s_act)
                bank_ev[bk_i] = (s_act, ic)
                i_c[o] = ic
            a_c = [None] * DC
            em.wait("scalar", s_dve, coef_ready)
            for o in range(DC):
                a_c[o] = em.op("scalar", lambda e, o_=Ac(o, tau), i=SCR1[:, o * T2:(o + 1) * T2],
                               cf=COEF[:, o:o + 1]:
                               e.activation(o_, i, AF.Exp, scale=cf), inc=s_act)
                a_counts[(tau, o)] = a_c[o]
            asq_c = [None] * DC
            for o in range(DC):
                em.wait("vector", s_act, a_c[o])
                asq_c[o] = em.op("vector", lambda e, o_=SCR1[:, o * T2:(o + 1) * T2],
                                 a=Ac(o, tau): e.tensor_tensor(o_, a, a, ALU.mult), inc=s_dve)
            z_c = [None] * DC
            for o in range(DC):
                em.wait("scalar", s_dve, asq_c[o])
                em.op("scalar", lambda e, o_=SCR3[:, o * T2:(o + 1) * T2],
                      i=SCR1[:, o * T2:(o + 1) * T2]:
                      e.activation(o_, i, AF.Ln, scale=-1.0, bias=ONE1[:, :]))
                z_c[o] = em.op("scalar", lambda e, o_=SCR1[:, o * T2:(o + 1) * T2],
                               i=SCR3[:, o * T2:(o + 1) * T2]:
                               e.activation(o_, i, AF.Exp, scale=0.5), inc=s_act)
            t_c = [None] * DC
            for o in range(DC):
                em.wait("vector", s_act, z_c[o])
                t_c[o] = em.op("vector", lambda e, o_=SCR3[:, o * T2:(o + 1) * T2],
                               z=SCR1[:, o * T2:(o + 1) * T2], ii=SCR2[:, o * T2:(o + 1) * T2]:
                               e.tensor_tensor(o_, z, ii, ALU.mult), inc=s_dve)
            for og in range(DC // 4):
              slab, ld, rec = gload("wt_fc", og, 4, DC)
              for oc in range(4):
                o = og * 4 + oc
                bk = o % 8
                g = mm_group(bk, (lambda k, s=slab, oc=oc: s(oc, k)),
                             [Hc(k, tau) for k in range(DC)], wait_wq=ld)
                rec[2] = g
                evac_wait("vector", bk)
                em.wait("vector", s_dve, t_c[o])
                s_cnt = em.op("vector", lambda e, o_=Sc(o, tau), i=PS[bk][:, :],
                              bv=VB["b_fc"][:, o:o + 1], t=SCR3[:, o * T2:(o + 1) * T2]:
                              e.scalar_tensor_tensor(o_, i, bv, t, ALU.add, ALU.mult),
                              inc=s_dve)
                bank_ev[bk] = (s_dve, s_cnt)

        if p == 0 and DEBUG_DUMPS:
            dump("dbg_a", A[:, :], [(s_act, s_act.n)])
            dump("dbg_s", S[:, :], [(s_dve, s_dve.n)])
            em.wait("vector", s_dbg, s_dbg.n)
            em.wait("scalar", s_dbg, s_dbg.n)

        scan_c = [None] * DC
        for o in range(DC):
            em.wait("vector", s_act, max(a_counts[(0, o)], a_counts[(1, o)]))
            if hh == 0:
                scan_c[o] = em.op("vector", lambda e, o_=S[:, o * TT:(o + 1) * TT],
                                  a=A[:, o * TT:(o + 1) * TT]:
                                  e.tensor_tensor_scan(o_, a, o_, 0.0, ALU.mult, ALU.add),
                                  inc=s_dve)
                em.wait("vector", s_dve, scan_c[o])
                em.op("vector", lambda e, c=CARRY[:, o:o + 1], s=S[:, o * TT + TT - 1: o * TT + TT]:
                      e.tensor_copy(c, s))
            else:
                scan_c[o] = em.op("vector", lambda e, o_=S[:, o * TT:(o + 1) * TT],
                                  a=A[:, o * TT:(o + 1) * TT], c=CARRY[:, o:o + 1]:
                                  e.tensor_tensor_scan(o_, a, o_, c, ALU.mult, ALU.add),
                                  inc=s_dve)

        if p == 0 and DEBUG_DUMPS:
            dump("dbg_out", S[:, :], [(s_dve, s_dve.n)])
            em.wait("vector", s_dbg, s_dbg.n)
            em.wait("scalar", s_dbg, s_dbg.n)

        em.wait("scalar", s_dve, s_dve.n)
        gslot_last_read = [None] * 4
        A3 = A[:, :].rearrange("p (kc t) -> p kc t", kc=DC)
        for og in range(DC // 4):
          slab, ld, rec = gload("wt_gate", og, 4, DC)
          for oc in range(4):
            o = og * 4 + oc
            for tau in range(2):
                bk = (o % 4) * 2 + tau
                g = mm_group(bk, (lambda k, s=slab, oc=oc: s(oc, k)),
                             [Hc(k, tau) for k in range(DC)], wait_wq=ld)
                if tau == 1:
                    rec[2] = g
                slot = (o % 2) * 2 + tau
                evac_wait("scalar", bk)
                if gslot_last_read[slot] is not None:
                    em.wait("scalar", s_dve, gslot_last_read[slot])
                g_cnt = em.op("scalar", lambda e, o_=SCR3[:, slot * T2:(slot + 1) * T2],
                              i=PS[bk][:, :], bv=VB["b_gate"][:, o:o + 1]:
                              e.activation(o_, i, AF.Gelu, bias=bv), inc=s_act)
                bank_ev[bk] = (s_act, g_cnt)
                em.wait("vector", s_act, g_cnt)
                em.wait("vector", s_dve, scan_c[o])
                em.op("vector", lambda e, o_=DSs(0), xk=Xc(o, tau),
                      m=MT[:, tau * T2:(tau + 1) * T2]:
                      e.tensor_tensor(o_, xk, m, ALU.subtract))
                em.op("vector", lambda e, o_=DSs(1), d=DSs(0),
                      r=RST[:, tau * T2:(tau + 1) * T2]:
                      e.tensor_tensor(o_, d, r, ALU.mult))
                em.op("vector", lambda e, o_=DSs(0), g_=SCR3[:, slot * T2:(slot + 1) * T2],
                      u=Sc(o, tau): e.tensor_tensor(o_, g_, u, ALU.mult))
                c = em.op("vector", lambda e, o_=Ac(o, tau), a=DSs(0), bb=DSs(1):
                          e.tensor_tensor(o_, a, bb, ALU.add), inc=s_dve)
                gslot_last_read[slot] = c
        hs2_done = s_dve.n

        if p == 0 and DEBUG_DUMPS:
            dump("dbg_hs2", A[:, :], [(s_dve, hs2_done)])
            em.wait("vector", s_dbg, s_dbg.n)
            em.wait("scalar", s_dbg, s_dbg.n)

        h2_ready = [None, None]
        for tau in range(2):
            em.wait("scalar", s_dve, hs2_done)
            em.wait("scalar", s_pe, prev_stats)
            casts = em.op("scalar", lambda e, o=SCR1c, i=A3[:, :, tau * T2:(tau + 1) * T2]:
                          e.activation(o, i, AF.Square), inc=s_act)
            gQ2 = mm_group(4 + tau, lambda k: ONESB[:, :],
                           [cast_tile(SCR1_bf, k) for k in range(DC)],
                           wait_inputs=[(s_act, casts)])
            prev_stats = gQ2
            evac_wait("scalar", 4 + tau)
            em.op("scalar", lambda e, o=DSs(2), i=PS[4 + tau][:, :]:
                  e.activation(o, i, AF.Ln, scale=1.0 / D, bias=EPS6T[:, :]))
            c2_c = em.op("scalar", lambda e, o=CC[:, tau * T2:(tau + 1) * T2], i=DSs(2):
                         e.activation(o, i, AF.Exp, scale=-0.5), inc=s_act)
            bank_ev[4 + tau] = (s_act, c2_c)
            em.wait("vector", s_act, c2_c)
            for k in range(DC):
                inc = s_dve if k == DC - 1 else None
                c = em.op("vector", lambda e, o=Hc(k, tau), i=Ac(k, tau),
                          nk=VB["n2w"][:, k:k + 1], c2=CC[:, tau * T2:(tau + 1) * T2]:
                          e.scalar_tensor_tensor(o, i, nk, c2, ALU.mult, ALU.mult), inc=inc)
            h2_ready[tau] = c

        gslot_last_read = [None] * 4
        for og in range(MC // 4):
          slab_g, ld_g, rec_g = gload("wt_mg", og, 4, DC)
          slab_f, ld_f, rec_f = gload("wt_mf", og, 4, DC)
          for oc in range(4):
            o = og * 4 + oc
            if True:
                pass
            for tau in range(2):
                bk_g = (o % 2) * 2 + tau
                bk_f = 4 + (o % 2) * 2 + tau
                g_g = mm_group(bk_g, (lambda k, s=slab_g, oc=oc: s(oc, k)),
                               [Hc(k, tau) for k in range(DC)],
                               wait_wq=ld_g, wait_inputs=[(s_dve, h2_ready[tau])])
                if tau == 1:
                    rec_g[2] = g_g
                g_f = mm_group(bk_f, (lambda k, s=slab_f, oc=oc: s(oc, k)),
                               [Hc(k, tau) for k in range(DC)], wait_wq=ld_f)
                if tau == 1:
                    rec_f[2] = g_f
                slot = (o % 2) * 2 + tau
                evac_wait("scalar", bk_g)
                if gslot_last_read[slot] is not None:
                    em.wait("scalar", s_dve, gslot_last_read[slot])
                g2c = em.op("scalar", lambda e, o_=SCR3[:, slot * T2:(slot + 1) * T2],
                            i=PS[bk_g][:, :], bv=VB["b_mg"][:, o:o + 1]:
                            e.activation(o_, i, AF.Gelu, bias=bv), inc=s_act)
                bank_ev[bk_g] = (s_act, g2c)
                evac_wait("vector", bk_f)
                em.wait("vector", s_act, g2c)
                pc = em.op("vector", lambda e, o_=P_tile(o, tau), i=PS[bk_f][:, :],
                           bv=VB["b_mf"][:, o:o + 1], g_=SCR3[:, slot * T2:(slot + 1) * T2]:
                           e.scalar_tensor_tensor(o_, i, bv, g_, ALU.add, ALU.mult),
                           inc=s_dve)
                bank_ev[bk_f] = (s_dve, pc)
                gslot_last_read[slot] = pc
        p_done = s_dve.n

        for o in range(DC):
            slab, ld, rec = gload("wt_out", o, 1, MC)
            for tau in range(2):
                bk = (o % 4) * 2 + tau
                g = mm_group(bk, (lambda k, s=slab: s(0, k)),
                             [P_tile(k, tau) for k in range(MC)],
                             wait_wq=ld, wait_inputs=[(s_dve, p_done)])
                if tau == 1:
                    rec[2] = g
                evac_wait("vector", bk)
                em.op("vector", lambda e, o_=DSs(0), i=PS[bk][:, :],
                      bv=VB["b_out"][:, o:o + 1], h2=Ac(o, tau):
                      e.scalar_tensor_tensor(o_, i, bv, h2, ALU.add, ALU.add))
                xc = em.op("vector", lambda e, o_=Xc(o, tau), u=DSs(0):
                           e.tensor_tensor(o_, o_, u, ALU.add), inc=s_dve)
                bank_ev[bk] = (s_dve, xc)
        xupd_done = s_dve.n

        if p == 0 and DEBUG_DUMPS:
            dump("dbg_x1", X[:, :], [(s_dve, xupd_done)])
            em.wait("vector", s_dbg, s_dbg.n)
            em.wait("scalar", s_dbg, s_dbg.n)

        xf_ready = [None, None]
        for tau in range(2):
            em.wait("scalar", s_dve, xupd_done)
            em.wait("scalar", s_pe, prev_stats)
            em.op("scalar", lambda e, o=SCR1c, i=X3[:, :, tau * T2:(tau + 1) * T2]:
                  e.activation(o, i, AF.Copy))
            casts = em.op("scalar", lambda e, o=SCR2c, i=X3[:, :, tau * T2:(tau + 1) * T2]:
                          e.activation(o, i, AF.Square), inc=s_act)
            gS = mm_group(4, lambda k: ONESB[:, :],
                          [cast_tile(SCR1_bf, k) for k in range(DC)],
                          wait_inputs=[(s_act, casts)])
            gQ = mm_group(5, lambda k: ONESB[:, :],
                          [cast_tile(SCR2_bf, k) for k in range(DC)])
            prev_stats = max(gS, gQ)
            evac_wait("vector", 4)
            em.op("vector", lambda e, o=MT[:, tau * T2:(tau + 1) * T2], i=PS[4][:, :]:
                  e.tensor_scalar(o, i, 1.0 / D, None, ALU.mult))
            em.op("vector", lambda e, o=DSs(0), m=MT[:, tau * T2:(tau + 1) * T2]:
                  e.tensor_tensor(o, m, m, ALU.mult))
            evac_wait("vector", 5)
            var_c = em.op("vector", lambda e, o=DSs(1), i=PS[5][:, :], m=DSs(0):
                          e.scalar_tensor_tensor(o, i, 1.0 / D, m, ALU.mult, ALU.subtract),
                          inc=s_dve)
            bank_ev[4] = (s_dve, var_c)
            bank_ev[5] = (s_dve, var_c)
            em.wait("scalar", s_dve, var_c)
            em.op("scalar", lambda e, o=DSs(2), i=DSs(1):
                  e.activation(o, i, AF.Ln, bias=EPS5[:, :]))
            r2_c = em.op("scalar", lambda e, o=RST[:, tau * T2:(tau + 1) * T2], i=DSs(2):
                         e.activation(o, i, AF.Exp, scale=-0.5), inc=s_act)
            em.wait("vector", s_act, r2_c)
            for k in range(DC):
                em.op("vector", lambda e, o=DSs(3), xk=Xc(k, tau),
                      m=MT[:, tau * T2:(tau + 1) * T2]:
                      e.tensor_tensor(o, xk, m, ALU.subtract))
                inc = s_dve if k == DC - 1 else None
                c = em.op("vector", lambda e, o=Hc(k, tau), d=DSs(3),
                          r=RST[:, tau * T2:(tau + 1) * T2]:
                          e.tensor_tensor(o, d, r, ALU.mult), inc=inc)
            xf_ready[tau] = c

        dslot_last_read = [None] * 4
        for og in range(FC // 4):
          slab_1, ld_1, rec_1 = gload("wt_w1", og, 4, DC)
          slab_2, ld_2, rec_2 = gload("wt_w2", og, 4, DC)
          for oc in range(4):
            o = og * 4 + oc
            if True:
                pass
            for tau in range(2):
                bk_1 = (o % 2) * 2 + tau
                bk_2 = 4 + (o % 2) * 2 + tau
                g_1 = mm_group(bk_1, (lambda k, s=slab_1, oc=oc: s(oc, k)),
                               [Hc(k, tau) for k in range(DC)],
                               wait_wq=ld_1, wait_inputs=[(s_dve, xf_ready[tau])])
                if tau == 1:
                    rec_1[2] = g_1
                g_2 = mm_group(bk_2, (lambda k, s=slab_2, oc=oc: s(oc, k)),
                               [Hc(k, tau) for k in range(DC)], wait_wq=ld_2)
                if tau == 1:
                    rec_2[2] = g_2
                slot = (o % 2) * 2 + tau
                evac_wait("scalar", bk_1)
                if dslot_last_read[slot] is not None:
                    em.wait("scalar", s_dve, dslot_last_read[slot])
                sl_c = em.op("scalar", lambda e, o_=DSs(slot), i=PS[bk_1][:, :],
                             sv=VB["s_w1"][:, o:o + 1]:
                             e.activation(o_, i, AF.Silu, scale=sv), inc=s_act)
                bank_ev[bk_1] = (s_act, sl_c)
                evac_wait("vector", bk_2)
                em.wait("vector", s_act, sl_c)
                tc = em.op("vector", lambda e, o_=T_tile(o, tau), i=PS[bk_2][:, :],
                           sv=VB["s_w2"][:, o:o + 1], si=DSs(slot):
                           e.scalar_tensor_tensor(o_, i, sv, si, ALU.mult, ALU.mult),
                           inc=s_dve)
                bank_ev[bk_2] = (s_dve, tc)
                dslot_last_read[slot] = tc
        t_done = s_dve.n

        for o in range(FC):
            slab, ld, rec = gload("wt_h1", o, 1, FC)
            for tau in range(2):
                bk = (o % 4) * 2 + tau
                g = mm_group(bk, (lambda k, s=slab: s(0, k)), [T_tile(k, tau) for k in range(FC)],
                             wait_wq=ld, wait_inputs=[(s_dve, t_done)])
                if tau == 1:
                    rec[2] = g
                evac_wait("scalar", bk)
                vc = em.op("scalar", lambda e, o_=V_tile(o, tau), i=PS[bk][:, :],
                           sv=VB["s_h1"][:, o:o + 1]:
                           e.activation(o_, i, AF.Gelu, scale=sv), inc=s_act)
                bank_ev[bk] = (s_act, vc)
        v_done = s_act.n

        for o in range(FC):
            slab, ld, rec = gload("wt_h2", o, 1, FC)
            for tau in range(2):
                bk = (o % 4) * 2 + tau
                g = mm_group(bk, (lambda k, s=slab: s(0, k)), [V_tile(k, tau) for k in range(FC)],
                             wait_wq=ld, wait_inputs=[(s_act, v_done)])
                if tau == 1:
                    rec[2] = g
                evac_wait("scalar", bk)
                wc = em.op("scalar", lambda e, o_=T_tile(o, tau), i=PS[bk][:, :],
                           sv=VB["s_h2"][:, o:o + 1]:
                           e.activation(o_, i, AF.Gelu, scale=sv), inc=s_act)
                bank_ev[bk] = (s_act, wc)
        w_done = s_act.n

        for o in range(DC):
            slab, ld, rec = gload("wt_w3", o, 1, FC)
            for tau in range(2):
                bk = (o % 4) * 2 + tau
                g = mm_group(bk, (lambda k, s=slab: s(0, k)), [T_tile(k, tau) for k in range(FC)],
                             wait_wq=ld, wait_inputs=[(s_act, w_done)])
                if tau == 1:
                    rec[2] = g
                evac_wait("vector", bk)
                xc = em.op("vector", lambda e, o_=Xc(o, tau), i=PS[bk][:, :],
                           sv=VB["s_w3"][:, o:o + 1]:
                           e.scalar_tensor_tensor(o_, i, sv, o_, ALU.mult, ALU.add),
                           inc=s_dve)
                bank_ev[bk] = (s_dve, xc)
            em.wait("sync", s_dve, xc)
            dsty = y_h[b, o * P:(o + 1) * P, hh * TT:(hh + 1) * TT]
            em.op("sync", lambda e, d=dsty, s=X[:, o * TT:(o + 1) * TT]:
                  e.dma_start(out=d, in_=s), inc=s_y, amount=16)

    em.wait("sync", s_y, 16 * DC * NPASS)

    with nc.Block() as block:
        @block.tensor
        def _(e):
            for fn in em.q["tensor"]:
                fn(e)

        @block.scalar
        def _(e):
            for fn in em.q["scalar"]:
                fn(e)

        @block.vector
        def _(e):
            for fn in em.q["vector"]:
                fn(e)

        @block.gpsimd
        def _(e):
            for fn in em.q["gpsimd"]:
                fn(e)

        @block.sync
        def _(e):
            for fn in em.q["sync"]:
                fn(e)

    es.close()
    return nc


def _prep_vec(v, c):
    v = np.ascontiguousarray(np.asarray(v, dtype=np.float32).reshape(c, P).T)
    return v


def kernel(**inputs):
    x = np.asarray(inputs["L0_attn_hidden_states"], dtype=np.float32)
    x_fm = np.ascontiguousarray(x.transpose(0, 2, 1))

    g = lambda n: np.asarray(inputs[n], dtype=np.float32)
    wT = lambda n: np.ascontiguousarray(g(n).T)

    common = {
        "wt_fc": wT("L0_attn_fc_w"), "wt_r": wT("L0_attn_fc_r_w"),
        "wt_i": wT("L0_attn_fc_i_w"), "wt_gate": wT("L0_attn_fc_gate_w"),
        "wt_mg": wT("L0_attn_mlp_gate_w"), "wt_mf": wT("L0_attn_mlp_fc_w"),
        "wt_out": wT("L0_attn_out_w"),
        "wt_w1": wT("L0_ffn_w1_w"), "wt_w2": wT("L0_ffn_w2_w"),
        "wt_h1": wT("L0_ffn_h1_w"), "wt_h2": wT("L0_ffn_h2_w"),
        "wt_w3": wT("L0_ffn_w3_w"),
        "b_fc": _prep_vec(g("L0_attn_fc_b"), DC),
        "b_r": _prep_vec(g("L0_attn_fc_r_b"), DC),
        "b_i": _prep_vec(g("L0_attn_fc_i_b"), DC),
        "b_gate": _prep_vec(g("L0_attn_fc_gate_b"), DC),
        "b_mg": _prep_vec(g("L0_attn_mlp_gate_b"), MC),
        "b_mf": _prep_vec(g("L0_attn_mlp_fc_b"), MC),
        "b_out": _prep_vec(g("L0_attn_out_b"), DC),
        "nw": _prep_vec(g("L0_attn_norm_w"), DC),
        "n2w": _prep_vec(g("L0_attn_norm2_w"), DC),
        "alv": _prep_vec(g("L0_attn_alpha").reshape(-1), DC),
        "s_w1": _prep_vec(g("L0_ffn_w1_s"), FC),
        "s_w2": _prep_vec(g("L0_ffn_w2_s"), FC),
        "s_h1": _prep_vec(g("L0_ffn_h1_s"), FC),
        "s_h2": _prep_vec(g("L0_ffn_h2_s"), FC),
        "s_w3": _prep_vec(g("L0_ffn_w3_s"), DC),
    }

    nc = build_program()
    in_maps = []
    for c in range(NCORES):
        m = dict(common)
        m["x"] = np.ascontiguousarray(x_fm[c * NB:(c + 1) * NB])
        in_maps.append(m)
    res = run_bass_kernel_spmd(nc, in_maps, core_ids=list(range(NCORES)))
    y_fm = np.concatenate([r["y"] for r in res.results], axis=0)
    return np.ascontiguousarray(y_fm.transpose(0, 2, 1))


# version 11
# speedup vs baseline: 1.2348x; 1.2348x over previous
import numpy as np
from contextlib import ExitStack
import concourse.bass as bass
import concourse.mybir as mybir
from concourse.bass_utils import run_bass_kernel_spmd

F32 = mybir.dt.float32
BF16 = mybir.dt.bfloat16
AF = mybir.ActivationFunctionType
ALU = mybir.AluOpType

P = 128
D = 1024
DC = 8
MD = 3072
MC = 24
FD = 4096
FC = 32
L = 2048
NB = 4
NCORES = 8
TT = 1024
T2 = 512
NPASS = NB * (L // TT)
EPS_LN = 1e-5
EPS_RMS1 = 1e-6
EPS_RMS2 = 1e-6

WRCOLS = 13312


class Ctr:

    def __init__(self, handle, name):
        self.h = handle
        self.name = name
        self.n = 0

    def inc(self, amount=1):
        self.n += amount
        return self.n


class Emitter:
    def __init__(self, nc):
        self.nc = nc
        self.q = {"tensor": [], "scalar": [], "vector": [], "gpsimd": [], "sync": []}
        self.waited = {e: {} for e in self.q}

    def wait(self, eng, ctr, val=None):
        v = ctr.n if val is None else val
        if v <= 0:
            return
        w = self.waited[eng]
        if w.get(ctr.name, -1) >= v:
            return
        w[ctr.name] = v
        h = ctr.h
        self.q[eng].append(lambda e, h=h, v=v: e.wait_ge(h, v))

    def op(self, eng, fn, inc=None, amount=1):
        if inc is None:
            self.q[eng].append(fn)
            return None
        val = inc.inc(amount)
        h = inc.h

        def run(e, fn=fn, h=h, amount=amount):
            fn(e).then_inc(h, amount)

        self.q[eng].append(run)
        return val


class Ring:

    def __init__(self, em, wr, s_wq, s_pe):
        self.em = em
        self.wr = wr
        self.s_wq = s_wq
        self.s_pe = s_pe
        self.pos = 0
        self.active = []

    def load(self, src_ap, kc):
        need = kc * P
        if self.pos + need > WRCOLS:
            self.pos = 0
        start, end = self.pos, self.pos + need
        self.pos = end
        overl = [r for r in self.active if r[0] < end and r[1] > start]
        for r in overl:
            assert r[2] is not None, "ring reuse before consume registered"
            self.em.wait("gpsimd", self.s_pe, r[2])
            self.active.remove(r)
        dst = self.wr[:, start:end].rearrange("p (kc o) -> p kc o", kc=kc)
        loaded = self.em.op(
            "gpsimd",
            lambda e, dst=dst, src=src_ap: e.dma_start(out=dst, in_=src),
            inc=self.s_wq, amount=16)
        rec = [start, end, None, loaded]
        self.active.append(rec)
        wr = self.wr

        def slab(k, start=start):
            return wr[:, start + k * P: start + (k + 1) * P]

        return slab, loaded, rec


DEBUG_DUMPS = False


def build_program():
    nc = bass.Bass("TRN2", target_bir_lowering=False)

    x_h = nc.dram_tensor("x", [NB, D, L], F32, kind="ExternalInput").ap()
    y_h = nc.dram_tensor("y", [NB, D, L], F32, kind="ExternalOutput").ap()

    wts = {}
    for name, k, o in [
        ("wt_fc", D, D), ("wt_r", D, D), ("wt_i", D, D), ("wt_gate", D, D),
        ("wt_mg", D, MD), ("wt_mf", D, MD), ("wt_out", MD, D),
        ("wt_w1", D, FD), ("wt_w2", D, FD), ("wt_h1", FD, FD), ("wt_h2", FD, FD),
        ("wt_w3", FD, D),
    ]:
        wts[name] = nc.dram_tensor(name, [k, o], F32, kind="ExternalInput").ap()

    vecs = {}
    for name, c in [
        ("b_fc", DC), ("b_r", DC), ("b_i", DC), ("b_gate", DC),
        ("b_mg", MC), ("b_mf", MC), ("b_out", DC),
        ("nw", DC), ("n2w", DC), ("alv", DC),
        ("s_w1", FC), ("s_w2", FC), ("s_h1", FC), ("s_h2", FC), ("s_w3", DC),
    ]:
        vecs[name] = nc.dram_tensor(name, [P, c], F32, kind="ExternalInput").ap()

    es = ExitStack()
    sb = lambda name, cols, dt=F32: es.enter_context(nc.sbuf_tensor(name, [P, cols], dt))

    X = sb("X", DC * TT)
    H = sb("H", DC * TT, BF16)
    A = sb("A", DC * TT)
    S = sb("S", DC * TT)
    SCR1 = sb("SCR1", 4 * TT)
    SCR2 = sb("SCR2", 4 * TT)
    SCR3 = sb("SCR3", 4 * TT)
    MT = sb("MT", TT)
    RST = sb("RST", TT)
    CC = sb("CC", TT)
    DS = sb("DS", 4 * T2)
    WR = sb("WR", WRCOLS, BF16)
    ONESB = sb("ONESB", P, BF16)
    COEF = sb("COEF", DC)
    CARRY = sb("CARRY", DC)
    ALV = sb("ALV", DC)
    EPS5 = sb("EPS5", 1)
    EPS11 = sb("EPS11", 1)
    EPS6T = sb("EPS6T", 1)
    ONE1 = sb("ONE1", 1)
    VB = {}
    for name, c in [("b_fc", DC), ("b_r", DC), ("b_i", DC), ("b_gate", DC),
                    ("b_mg", MC), ("b_mf", MC), ("b_out", DC), ("nw", DC),
                    ("n2w", DC), ("s_w1", FC), ("s_w2", FC), ("s_h1", FC),
                    ("s_h2", FC), ("s_w3", DC)]:
        VB[name] = sb("v_" + name, c)

    PS = [es.enter_context(nc.psum_tensor(f"ps{i}", [P, T2], F32)) for i in range(8)]

    s_pe = Ctr(es.enter_context(nc.semaphore("s_pe")), "s_pe")
    s_act = Ctr(es.enter_context(nc.semaphore("s_act")), "s_act")
    s_dve = Ctr(es.enter_context(nc.semaphore("s_dve")), "s_dve")
    s_wq = Ctr(es.enter_context(nc.semaphore("s_wq")), "s_wq")
    s_x = Ctr(es.enter_context(nc.semaphore("s_x")), "s_x")
    s_y = Ctr(es.enter_context(nc.semaphore("s_y")), "s_y")

    em = Emitter(nc)
    ring = Ring(em, WR, s_wq, s_pe)

    dbg = {}
    s_dbg = Ctr(es.enter_context(nc.semaphore("s_dbg")), "s_dbg") if DEBUG_DUMPS else None
    if DEBUG_DUMPS:
        for name, cols, dt in [("dbg_h", DC * TT, BF16), ("dbg_a", DC * TT, F32),
                               ("dbg_s", DC * TT, F32), ("dbg_out", DC * TT, F32),
                               ("dbg_x1", DC * TT, F32), ("dbg_hs2", DC * TT, F32),
                               ("dbg_stats", 4 * TT, F32)]:
            dbg[name] = nc.dram_tensor(name, [P, cols], dt, kind="ExternalOutput").ap()

    def dump(name, src_ap, wait_list):
        if not DEBUG_DUMPS:
            return None
        for ctr, cnt in wait_list:
            em.wait("sync", ctr, cnt)
        return em.op("sync", lambda e, d=dbg[name], s=src_ap: e.dma_start(out=d, in_=s),
                     inc=s_dbg, amount=16)

    bank_mm = [0] * 8
    bank_ev = [None] * 8

    def Hc(k, tau):
        return H[:, k * TT + tau * T2: k * TT + (tau + 1) * T2]

    def Xc(k, tau):
        return X[:, k * TT + tau * T2: k * TT + (tau + 1) * T2]

    def Ac(k, tau):
        return A[:, k * TT + tau * T2: k * TT + (tau + 1) * T2]

    def Sc(k, tau):
        return S[:, k * TT + tau * T2: k * TT + (tau + 1) * T2]

    S_bf = S[:, :].bitcast(BF16)
    A_bf = A[:, :].bitcast(BF16)
    SCR1_bf = SCR1[:, :].bitcast(BF16)
    SCR2_bf = SCR2[:, :].bitcast(BF16)
    SCR3_bf = SCR3[:, :].bitcast(BF16)
    H_bf = H

    def P_tile(j, tau):
        idx = j * 2 + tau
        if idx < 32:
            return S_bf[:, idx * T2:(idx + 1) * T2]
        return SCR1_bf[:, (idx - 32) * T2:(idx - 31) * T2]

    def T_tile(j, tau):
        idx = j * 2 + tau
        if idx < 32:
            return A_bf[:, idx * T2:(idx + 1) * T2]
        return S_bf[:, (idx - 32) * T2:(idx - 31) * T2]

    def V_tile(j, tau):
        idx = j * 2 + tau
        if idx < 16:
            return SCR1_bf[:, idx * T2:(idx + 1) * T2]
        if idx < 32:
            return SCR2_bf[:, (idx - 16) * T2:(idx - 15) * T2]
        if idx < 48:
            return SCR3_bf[:, (idx - 32) * T2:(idx - 31) * T2]
        return H_bf[:, (idx - 48) * T2:(idx - 47) * T2]

    def DSs(i):
        return DS[:, i * T2:(i + 1) * T2]

    def cast_tile(buf_bf, k):
        return buf_bf[:, k * T2:(k + 1) * T2]

    def mm_group(bank, lhsT_fn, rhs_list, wait_wq=None, wait_inputs=()):
        if bank_ev[bank] is not None:
            ctr, cnt = bank_ev[bank]
            em.wait("tensor", ctr, cnt)
        if wait_wq is not None:
            em.wait("tensor", s_wq, wait_wq)
        for ctr, cnt in wait_inputs:
            em.wait("tensor", ctr, cnt)
        nk = len(rhs_list)
        for k in range(nk):
            lhsT = lhsT_fn(k)
            rhs = rhs_list[k]
            if k == nk - 1:
                cnt = em.op("tensor",
                            lambda e, o=PS[bank][:, :], l=lhsT, r=rhs, st=(k == 0):
                            e.matmul(o, l, r, start=st, stop=True),
                            inc=s_pe)
            else:
                em.op("tensor",
                      lambda e, o=PS[bank][:, :], l=lhsT, r=rhs, st=(k == 0):
                      e.matmul(o, l, r, start=st, stop=False))
        bank_mm[bank] = cnt
        return cnt

    def evac_wait(eng, bank):
        em.wait(eng, s_pe, bank_mm[bank])

    for name in VB:
        em.op("sync", lambda e, d=VB[name][:, :], s=vecs[name]: e.dma_start(out=d, in_=s),
              inc=s_x, amount=16)
    em.op("sync", lambda e: e.dma_start(out=ALV[:, :], in_=vecs["alv"]), inc=s_x, amount=16)
    vecs_loaded = s_x.n

    em.op("vector", lambda e: e.memset(ONESB[:, :], 1.0))
    em.op("vector", lambda e: e.memset(EPS5[:, :], EPS_LN))
    em.op("vector", lambda e: e.memset(EPS11[:, :], EPS_LN * EPS_RMS1))
    em.op("vector", lambda e: e.memset(EPS6T[:, :], EPS_RMS2))
    cfg_dve = em.op("vector", lambda e: e.memset(ONE1[:, :], 1.0), inc=s_dve)

    em.wait("scalar", s_x, vecs_loaded)
    em.wait("scalar", s_dve, cfg_dve)
    c = em.op("scalar", lambda e: e.activation(DS[:, 0:DC], ALV[:, :], AF.Abs), inc=s_act)
    em.wait("scalar", s_act, c)
    c = em.op("scalar", lambda e: e.activation(DS[:, DC:2 * DC], DS[:, 0:DC], AF.Exp,
                                               scale=-1.0), inc=s_act)
    em.wait("scalar", s_act, c)
    c_sp = em.op("scalar", lambda e: e.activation(DS[:, 0:DC], DS[:, DC:2 * DC], AF.Ln,
                                                  bias=ONE1[:, :]), inc=s_act)
    em.wait("vector", s_x, vecs_loaded)
    c = em.op("vector", lambda e: e.tensor_scalar(DS[:, 3 * DC:4 * DC], ALV[:, :], 0.0, None,
                                                  ALU.max), inc=s_dve)
    em.wait("vector", s_dve, c)
    em.wait("vector", s_act, c_sp)
    c = em.op("vector", lambda e: e.tensor_tensor(DS[:, 2 * DC:3 * DC], DS[:, 0:DC],
                                                  DS[:, 3 * DC:4 * DC], ALU.add), inc=s_dve)
    em.wait("vector", s_dve, c)
    coef_ready = em.op("vector", lambda e: e.tensor_scalar(COEF[:, :], DS[:, 2 * DC:3 * DC],
                                                           -8.0, None, ALU.mult), inc=s_dve)

    for p in range(NPASS):
        b, hh = divmod(p, L // TT)

        for o in range(DC):
            if p > 0:
                em.wait("sync", s_y, 16 * ((p - 1) * DC + o + 1))
            src = x_h[b, o * P:(o + 1) * P, hh * TT:(hh + 1) * TT]
            em.op("sync", lambda e, d=X[:, o * TT:(o + 1) * TT], s=src:
                  e.dma_start(out=d, in_=s), inc=s_x, amount=16)
        xload = s_x.n

        em.wait("scalar", s_dve, s_dve.n)
        em.wait("vector", s_act, s_act.n)
        em.wait("scalar", s_pe, s_pe.n)
        em.wait("vector", s_pe, s_pe.n)

        X3 = X[:, :].rearrange("p (kc t) -> p kc t", kc=DC)
        SCR1c = SCR1_bf[:, 0:DC * T2].rearrange("p (kc t) -> p kc t", kc=DC)
        SCR2c = SCR2_bf[:, 0:DC * T2].rearrange("p (kc t) -> p kc t", kc=DC)
        h_ready = [None, None]
        prev_stats = 0
        for tau in range(2):
            em.wait("scalar", s_x, xload)
            em.wait("scalar", s_pe, prev_stats)
            em.op("scalar", lambda e, o=SCR1c, i=X3[:, :, tau * T2:(tau + 1) * T2]:
                  e.activation(o, i, AF.Copy))
            casts = em.op("scalar", lambda e, o=SCR2c, i=X3[:, :, tau * T2:(tau + 1) * T2]:
                          e.activation(o, i, AF.Square), inc=s_act)
            em.wait("tensor", s_dve, cfg_dve)
            gS = mm_group(4, lambda k: ONESB[:, :],
                          [cast_tile(SCR1_bf, k) for k in range(DC)],
                          wait_inputs=[(s_act, casts)])
            gQ = mm_group(5, lambda k: ONESB[:, :],
                          [cast_tile(SCR2_bf, k) for k in range(DC)])
            prev_stats = max(gS, gQ)
            evac_wait("vector", 4)
            em.op("vector", lambda e, o=MT[:, tau * T2:(tau + 1) * T2], i=PS[4][:, :]:
                  e.tensor_scalar(o, i, 1.0 / D, None, ALU.mult))
            em.op("vector", lambda e, o=DSs(0), m=MT[:, tau * T2:(tau + 1) * T2]:
                  e.tensor_tensor(o, m, m, ALU.mult))
            evac_wait("vector", 5)
            var_c = em.op("vector", lambda e, o=DSs(1), i=PS[5][:, :], m=DSs(0):
                          e.scalar_tensor_tensor(o, i, 1.0 / D, m, ALU.mult, ALU.subtract),
                          inc=s_dve)
            bank_ev[4] = (s_dve, var_c)
            bank_ev[5] = (s_dve, var_c)
            em.wait("scalar", s_dve, var_c)
            em.op("scalar", lambda e, o=DSs(2), i=DSs(1):
                  e.activation(o, i, AF.Ln, bias=EPS5[:, :]))
            em.op("scalar", lambda e, o=RST[:, tau * T2:(tau + 1) * T2], i=DSs(2):
                  e.activation(o, i, AF.Exp, scale=-0.5))
            em.op("scalar", lambda e, o=DSs(2), i=DSs(1):
                  e.activation(o, i, AF.Ln, scale=1.0 + EPS_RMS1, bias=EPS11[:, :]))
            c1_c = em.op("scalar", lambda e, o=CC[:, tau * T2:(tau + 1) * T2], i=DSs(2):
                         e.activation(o, i, AF.Exp, scale=-0.5), inc=s_act)
            em.wait("vector", s_act, c1_c)
            for k in range(DC):
                em.op("vector", lambda e, o=DSs(3), xk=Xc(k, tau), m=MT[:, tau * T2:(tau + 1) * T2]:
                      e.tensor_tensor(o, xk, m, ALU.subtract))
                inc = s_dve if k == DC - 1 else None
                c = em.op("vector", lambda e, o=Hc(k, tau), d=DSs(3),
                          nwk=VB["nw"][:, k:k + 1], c1=CC[:, tau * T2:(tau + 1) * T2]:
                          e.scalar_tensor_tensor(o, d, nwk, c1, ALU.mult, ALU.mult), inc=inc)
            h_ready[tau] = c

        if p == 0 and DEBUG_DUMPS:
            c = dump("dbg_h", H[:, :], [(s_dve, h_ready[1])])
            em.op("sync", lambda e, d=dbg["dbg_stats"][:, 0:TT], s=MT[:, :]:
                  e.dma_start(out=d, in_=s), inc=s_dbg, amount=16)
            em.op("sync", lambda e, d=dbg["dbg_stats"][:, TT:2 * TT], s=RST[:, :]:
                  e.dma_start(out=d, in_=s), inc=s_dbg, amount=16)
            em.op("sync", lambda e, d=dbg["dbg_stats"][:, 2 * TT:3 * TT], s=CC[:, :]:
                  e.dma_start(out=d, in_=s), inc=s_dbg, amount=16)
            em.wait("vector", s_dbg, s_dbg.n)
            em.wait("scalar", s_dbg, s_dbg.n)

        a_counts = {}
        for tau in range(2):
            em.wait("scalar", s_dve, s_dve.n)
            sig_c = [None] * DC
            i_c = [None] * DC
            for o in range(DC):
                slab_r, ld_r, rec_r = ring.load(
                    wts["wt_r"][:, o * P:(o + 1) * P].rearrange("(kc kp) o -> kp kc o", kp=P), DC)
                slab_i, ld_i, rec_i = ring.load(
                    wts["wt_i"][:, o * P:(o + 1) * P].rearrange("(kc kp) o -> kp kc o", kp=P), DC)
                bk_r = o % 2
                bk_i = 2 + o % 2
                g_r = mm_group(bk_r, slab_r, [Hc(k, tau) for k in range(DC)],
                               wait_wq=ld_r, wait_inputs=[(s_dve, h_ready[tau])])
                rec_r[2] = g_r
                g_i = mm_group(bk_i, slab_i, [Hc(k, tau) for k in range(DC)], wait_wq=ld_i)
                rec_i[2] = g_i
                em.wait("scalar", s_pe, max(gS, gQ))
                evac_wait("scalar", bk_r)
                r_cnt = em.op("scalar", lambda e, o_=SCR1[:, o * T2:(o + 1) * T2],
                              i=PS[bk_r][:, :], bv=VB["b_r"][:, o:o + 1]:
                              e.activation(o_, i, AF.Sigmoid, bias=bv), inc=s_act)
                bank_ev[bk_r] = (s_act, r_cnt)
                sig_c[o] = r_cnt
                evac_wait("scalar", bk_i)
                ic = em.op("scalar", lambda e, o_=SCR2[:, o * T2:(o + 1) * T2],
                           i=PS[bk_i][:, :], bv=VB["b_i"][:, o:o + 1]:
                           e.activation(o_, i, AF.Sigmoid, bias=bv), inc=s_act)
                bank_ev[bk_i] = (s_act, ic)
                i_c[o] = ic
            a_c = [None] * DC
            em.wait("scalar", s_dve, coef_ready)
            for o in range(DC):
                a_c[o] = em.op("scalar", lambda e, o_=Ac(o, tau), i=SCR1[:, o * T2:(o + 1) * T2],
                               cf=COEF[:, o:o + 1]:
                               e.activation(o_, i, AF.Exp, scale=cf), inc=s_act)
                a_counts[(tau, o)] = a_c[o]
            asq_c = [None] * DC
            for o in range(DC):
                em.wait("vector", s_act, a_c[o])
                asq_c[o] = em.op("vector", lambda e, o_=SCR1[:, o * T2:(o + 1) * T2],
                                 a=Ac(o, tau): e.tensor_tensor(o_, a, a, ALU.mult), inc=s_dve)
            z_c = [None] * DC
            for o in range(DC):
                em.wait("scalar", s_dve, asq_c[o])
                em.op("scalar", lambda e, o_=SCR3[:, o * T2:(o + 1) * T2],
                      i=SCR1[:, o * T2:(o + 1) * T2]:
                      e.activation(o_, i, AF.Ln, scale=-1.0, bias=ONE1[:, :]))
                z_c[o] = em.op("scalar", lambda e, o_=SCR1[:, o * T2:(o + 1) * T2],
                               i=SCR3[:, o * T2:(o + 1) * T2]:
                               e.activation(o_, i, AF.Exp, scale=0.5), inc=s_act)
            t_c = [None] * DC
            for o in range(DC):
                em.wait("vector", s_act, z_c[o])
                t_c[o] = em.op("vector", lambda e, o_=SCR3[:, o * T2:(o + 1) * T2],
                               z=SCR1[:, o * T2:(o + 1) * T2], ii=SCR2[:, o * T2:(o + 1) * T2]:
                               e.tensor_tensor(o_, z, ii, ALU.mult), inc=s_dve)
            for o in range(DC):
                slab, ld, rec = ring.load(
                    wts["wt_fc"][:, o * P:(o + 1) * P].rearrange("(kc kp) o -> kp kc o", kp=P), DC)
                bk = o % 8
                g = mm_group(bk, slab, [Hc(k, tau) for k in range(DC)], wait_wq=ld)
                rec[2] = g
                evac_wait("vector", bk)
                em.wait("vector", s_dve, t_c[o])
                s_cnt = em.op("vector", lambda e, o_=Sc(o, tau), i=PS[bk][:, :],
                              bv=VB["b_fc"][:, o:o + 1], t=SCR3[:, o * T2:(o + 1) * T2]:
                              e.scalar_tensor_tensor(o_, i, bv, t, ALU.add, ALU.mult),
                              inc=s_dve)
                bank_ev[bk] = (s_dve, s_cnt)

        if p == 0 and DEBUG_DUMPS:
            dump("dbg_a", A[:, :], [(s_act, s_act.n)])
            dump("dbg_s", S[:, :], [(s_dve, s_dve.n)])
            em.wait("vector", s_dbg, s_dbg.n)
            em.wait("scalar", s_dbg, s_dbg.n)

        scan_c = [None] * DC
        for o in range(DC):
            em.wait("vector", s_act, max(a_counts[(0, o)], a_counts[(1, o)]))
            if hh == 0:
                scan_c[o] = em.op("vector", lambda e, o_=S[:, o * TT:(o + 1) * TT],
                                  a=A[:, o * TT:(o + 1) * TT]:
                                  e.tensor_tensor_scan(o_, a, o_, 0.0, ALU.mult, ALU.add),
                                  inc=s_dve)
                em.wait("vector", s_dve, scan_c[o])
                em.op("vector", lambda e, c=CARRY[:, o:o + 1], s=S[:, o * TT + TT - 1: o * TT + TT]:
                      e.tensor_copy(c, s))
            else:
                scan_c[o] = em.op("vector", lambda e, o_=S[:, o * TT:(o + 1) * TT],
                                  a=A[:, o * TT:(o + 1) * TT], c=CARRY[:, o:o + 1]:
                                  e.tensor_tensor_scan(o_, a, o_, c, ALU.mult, ALU.add),
                                  inc=s_dve)

        if p == 0 and DEBUG_DUMPS:
            dump("dbg_out", S[:, :], [(s_dve, s_dve.n)])
            em.wait("vector", s_dbg, s_dbg.n)
            em.wait("scalar", s_dbg, s_dbg.n)

        em.wait("scalar", s_dve, s_dve.n)
        gslot_last_read = [None] * 4
        A3 = A[:, :].rearrange("p (kc t) -> p kc t", kc=DC)
        for o in range(DC):
            slab, ld, rec = ring.load(
                wts["wt_gate"][:, o * P:(o + 1) * P].rearrange("(kc kp) o -> kp kc o", kp=P), DC)
            for tau in range(2):
                bk = (o % 4) * 2 + tau
                g = mm_group(bk, slab, [Hc(k, tau) for k in range(DC)], wait_wq=ld)
                if tau == 1:
                    rec[2] = g
                slot = (o % 2) * 2 + tau
                evac_wait("scalar", bk)
                if gslot_last_read[slot] is not None:
                    em.wait("scalar", s_dve, gslot_last_read[slot])
                g_cnt = em.op("scalar", lambda e, o_=SCR3[:, slot * T2:(slot + 1) * T2],
                              i=PS[bk][:, :], bv=VB["b_gate"][:, o:o + 1]:
                              e.activation(o_, i, AF.Gelu, bias=bv), inc=s_act)
                bank_ev[bk] = (s_act, g_cnt)
                em.wait("vector", s_act, g_cnt)
                em.wait("vector", s_dve, scan_c[o])
                em.op("vector", lambda e, o_=DSs(0), xk=Xc(o, tau),
                      m=MT[:, tau * T2:(tau + 1) * T2]:
                      e.tensor_tensor(o_, xk, m, ALU.subtract))
                em.op("vector", lambda e, o_=DSs(1), d=DSs(0),
                      r=RST[:, tau * T2:(tau + 1) * T2]:
                      e.tensor_tensor(o_, d, r, ALU.mult))
                em.op("vector", lambda e, o_=DSs(0), g_=SCR3[:, slot * T2:(slot + 1) * T2],
                      u=Sc(o, tau): e.tensor_tensor(o_, g_, u, ALU.mult))
                c = em.op("vector", lambda e, o_=Ac(o, tau), a=DSs(0), bb=DSs(1):
                          e.tensor_tensor(o_, a, bb, ALU.add), inc=s_dve)
                gslot_last_read[slot] = c
        hs2_done = s_dve.n

        if p == 0 and DEBUG_DUMPS:
            dump("dbg_hs2", A[:, :], [(s_dve, hs2_done)])
            em.wait("vector", s_dbg, s_dbg.n)
            em.wait("scalar", s_dbg, s_dbg.n)

        h2_ready = [None, None]
        for tau in range(2):
            em.wait("scalar", s_dve, hs2_done)
            em.wait("scalar", s_pe, prev_stats)
            casts = em.op("scalar", lambda e, o=SCR1c, i=A3[:, :, tau * T2:(tau + 1) * T2]:
                          e.activation(o, i, AF.Square), inc=s_act)
            gQ2 = mm_group(4 + tau, lambda k: ONESB[:, :],
                           [cast_tile(SCR1_bf, k) for k in range(DC)],
                           wait_inputs=[(s_act, casts)])
            prev_stats = gQ2
            evac_wait("scalar", 4 + tau)
            em.op("scalar", lambda e, o=DSs(2), i=PS[4 + tau][:, :]:
                  e.activation(o, i, AF.Ln, scale=1.0 / D, bias=EPS6T[:, :]))
            c2_c = em.op("scalar", lambda e, o=CC[:, tau * T2:(tau + 1) * T2], i=DSs(2):
                         e.activation(o, i, AF.Exp, scale=-0.5), inc=s_act)
            bank_ev[4 + tau] = (s_act, c2_c)
            em.wait("vector", s_act, c2_c)
            for k in range(DC):
                inc = s_dve if k == DC - 1 else None
                c = em.op("vector", lambda e, o=Hc(k, tau), i=Ac(k, tau),
                          nk=VB["n2w"][:, k:k + 1], c2=CC[:, tau * T2:(tau + 1) * T2]:
                          e.scalar_tensor_tensor(o, i, nk, c2, ALU.mult, ALU.mult), inc=inc)
            h2_ready[tau] = c

        gslot_last_read = [None] * 4
        for o in range(MC):
            slab_g, ld_g, rec_g = ring.load(
                wts["wt_mg"][:, o * P:(o + 1) * P].rearrange("(kc kp) o -> kp kc o", kp=P), DC)
            slab_f, ld_f, rec_f = ring.load(
                wts["wt_mf"][:, o * P:(o + 1) * P].rearrange("(kc kp) o -> kp kc o", kp=P), DC)
            for tau in range(2):
                bk_g = (o % 2) * 2 + tau
                bk_f = 4 + (o % 2) * 2 + tau
                g_g = mm_group(bk_g, slab_g, [Hc(k, tau) for k in range(DC)],
                               wait_wq=ld_g, wait_inputs=[(s_dve, h2_ready[tau])])
                if tau == 1:
                    rec_g[2] = g_g
                g_f = mm_group(bk_f, slab_f, [Hc(k, tau) for k in range(DC)], wait_wq=ld_f)
                if tau == 1:
                    rec_f[2] = g_f
                slot = (o % 2) * 2 + tau
                evac_wait("scalar", bk_g)
                if gslot_last_read[slot] is not None:
                    em.wait("scalar", s_dve, gslot_last_read[slot])
                g2c = em.op("scalar", lambda e, o_=SCR3[:, slot * T2:(slot + 1) * T2],
                            i=PS[bk_g][:, :], bv=VB["b_mg"][:, o:o + 1]:
                            e.activation(o_, i, AF.Gelu, bias=bv), inc=s_act)
                bank_ev[bk_g] = (s_act, g2c)
                evac_wait("vector", bk_f)
                em.wait("vector", s_act, g2c)
                pc = em.op("vector", lambda e, o_=P_tile(o, tau), i=PS[bk_f][:, :],
                           bv=VB["b_mf"][:, o:o + 1], g_=SCR3[:, slot * T2:(slot + 1) * T2]:
                           e.scalar_tensor_tensor(o_, i, bv, g_, ALU.add, ALU.mult),
                           inc=s_dve)
                bank_ev[bk_f] = (s_dve, pc)
                gslot_last_read[slot] = pc
        p_done = s_dve.n

        for o in range(DC):
            slab, ld, rec = ring.load(
                wts["wt_out"][:, o * P:(o + 1) * P].rearrange("(kc kp) o -> kp kc o", kp=P), MC)
            for tau in range(2):
                bk = (o % 4) * 2 + tau
                g = mm_group(bk, slab, [P_tile(k, tau) for k in range(MC)],
                             wait_wq=ld, wait_inputs=[(s_dve, p_done)])
                if tau == 1:
                    rec[2] = g
                evac_wait("vector", bk)
                em.op("vector", lambda e, o_=DSs(0), i=PS[bk][:, :],
                      bv=VB["b_out"][:, o:o + 1], h2=Ac(o, tau):
                      e.scalar_tensor_tensor(o_, i, bv, h2, ALU.add, ALU.add))
                xc = em.op("vector", lambda e, o_=Xc(o, tau), u=DSs(0):
                           e.tensor_tensor(o_, o_, u, ALU.add), inc=s_dve)
                bank_ev[bk] = (s_dve, xc)
        xupd_done = s_dve.n

        if p == 0 and DEBUG_DUMPS:
            dump("dbg_x1", X[:, :], [(s_dve, xupd_done)])
            em.wait("vector", s_dbg, s_dbg.n)
            em.wait("scalar", s_dbg, s_dbg.n)

        xf_ready = [None, None]
        for tau in range(2):
            em.wait("scalar", s_dve, xupd_done)
            em.wait("scalar", s_pe, prev_stats)
            em.op("scalar", lambda e, o=SCR1c, i=X3[:, :, tau * T2:(tau + 1) * T2]:
                  e.activation(o, i, AF.Copy))
            casts = em.op("scalar", lambda e, o=SCR2c, i=X3[:, :, tau * T2:(tau + 1) * T2]:
                          e.activation(o, i, AF.Square), inc=s_act)
            gS = mm_group(4, lambda k: ONESB[:, :],
                          [cast_tile(SCR1_bf, k) for k in range(DC)],
                          wait_inputs=[(s_act, casts)])
            gQ = mm_group(5, lambda k: ONESB[:, :],
                          [cast_tile(SCR2_bf, k) for k in range(DC)])
            prev_stats = max(gS, gQ)
            evac_wait("vector", 4)
            em.op("vector", lambda e, o=MT[:, tau * T2:(tau + 1) * T2], i=PS[4][:, :]:
                  e.tensor_scalar(o, i, 1.0 / D, None, ALU.mult))
            em.op("vector", lambda e, o=DSs(0), m=MT[:, tau * T2:(tau + 1) * T2]:
                  e.tensor_tensor(o, m, m, ALU.mult))
            evac_wait("vector", 5)
            var_c = em.op("vector", lambda e, o=DSs(1), i=PS[5][:, :], m=DSs(0):
                          e.scalar_tensor_tensor(o, i, 1.0 / D, m, ALU.mult, ALU.subtract),
                          inc=s_dve)
            bank_ev[4] = (s_dve, var_c)
            bank_ev[5] = (s_dve, var_c)
            em.wait("scalar", s_dve, var_c)
            em.op("scalar", lambda e, o=DSs(2), i=DSs(1):
                  e.activation(o, i, AF.Ln, bias=EPS5[:, :]))
            r2_c = em.op("scalar", lambda e, o=RST[:, tau * T2:(tau + 1) * T2], i=DSs(2):
                         e.activation(o, i, AF.Exp, scale=-0.5), inc=s_act)
            em.wait("vector", s_act, r2_c)
            for k in range(DC):
                em.op("vector", lambda e, o=DSs(3), xk=Xc(k, tau),
                      m=MT[:, tau * T2:(tau + 1) * T2]:
                      e.tensor_tensor(o, xk, m, ALU.subtract))
                inc = s_dve if k == DC - 1 else None
                c = em.op("vector", lambda e, o=Hc(k, tau), d=DSs(3),
                          r=RST[:, tau * T2:(tau + 1) * T2]:
                          e.tensor_tensor(o, d, r, ALU.mult), inc=inc)
            xf_ready[tau] = c

        dslot_last_read = [None] * 4
        for o in range(FC):
            slab_1, ld_1, rec_1 = ring.load(
                wts["wt_w1"][:, o * P:(o + 1) * P].rearrange("(kc kp) o -> kp kc o", kp=P), DC)
            slab_2, ld_2, rec_2 = ring.load(
                wts["wt_w2"][:, o * P:(o + 1) * P].rearrange("(kc kp) o -> kp kc o", kp=P), DC)
            for tau in range(2):
                bk_1 = (o % 2) * 2 + tau
                bk_2 = 4 + (o % 2) * 2 + tau
                g_1 = mm_group(bk_1, slab_1, [Hc(k, tau) for k in range(DC)],
                               wait_wq=ld_1, wait_inputs=[(s_dve, xf_ready[tau])])
                if tau == 1:
                    rec_1[2] = g_1
                g_2 = mm_group(bk_2, slab_2, [Hc(k, tau) for k in range(DC)], wait_wq=ld_2)
                if tau == 1:
                    rec_2[2] = g_2
                slot = (o % 2) * 2 + tau
                evac_wait("scalar", bk_1)
                if dslot_last_read[slot] is not None:
                    em.wait("scalar", s_dve, dslot_last_read[slot])
                sl_c = em.op("scalar", lambda e, o_=DSs(slot), i=PS[bk_1][:, :],
                             sv=VB["s_w1"][:, o:o + 1]:
                             e.activation(o_, i, AF.Silu, scale=sv), inc=s_act)
                bank_ev[bk_1] = (s_act, sl_c)
                evac_wait("vector", bk_2)
                em.wait("vector", s_act, sl_c)
                tc = em.op("vector", lambda e, o_=T_tile(o, tau), i=PS[bk_2][:, :],
                           sv=VB["s_w2"][:, o:o + 1], si=DSs(slot):
                           e.scalar_tensor_tensor(o_, i, sv, si, ALU.mult, ALU.mult),
                           inc=s_dve)
                bank_ev[bk_2] = (s_dve, tc)
                dslot_last_read[slot] = tc
        t_done = s_dve.n

        for o in range(FC):
            slab, ld, rec = ring.load(
                wts["wt_h1"][:, o * P:(o + 1) * P].rearrange("(kc kp) o -> kp kc o", kp=P), FC)
            for tau in range(2):
                bk = (o % 4) * 2 + tau
                g = mm_group(bk, slab, [T_tile(k, tau) for k in range(FC)],
                             wait_wq=ld, wait_inputs=[(s_dve, t_done)])
                if tau == 1:
                    rec[2] = g
                evac_wait("scalar", bk)
                vc = em.op("scalar", lambda e, o_=V_tile(o, tau), i=PS[bk][:, :],
                           sv=VB["s_h1"][:, o:o + 1]:
                           e.activation(o_, i, AF.Gelu, scale=sv), inc=s_act)
                bank_ev[bk] = (s_act, vc)
        v_done = s_act.n

        for o in range(FC):
            slab, ld, rec = ring.load(
                wts["wt_h2"][:, o * P:(o + 1) * P].rearrange("(kc kp) o -> kp kc o", kp=P), FC)
            for tau in range(2):
                bk = (o % 4) * 2 + tau
                g = mm_group(bk, slab, [V_tile(k, tau) for k in range(FC)],
                             wait_wq=ld, wait_inputs=[(s_act, v_done)])
                if tau == 1:
                    rec[2] = g
                evac_wait("scalar", bk)
                wc = em.op("scalar", lambda e, o_=T_tile(o, tau), i=PS[bk][:, :],
                           sv=VB["s_h2"][:, o:o + 1]:
                           e.activation(o_, i, AF.Gelu, scale=sv), inc=s_act)
                bank_ev[bk] = (s_act, wc)
        w_done = s_act.n

        for o in range(DC):
            slab, ld, rec = ring.load(
                wts["wt_w3"][:, o * P:(o + 1) * P].rearrange("(kc kp) o -> kp kc o", kp=P), FC)
            for tau in range(2):
                bk = (o % 4) * 2 + tau
                g = mm_group(bk, slab, [T_tile(k, tau) for k in range(FC)],
                             wait_wq=ld, wait_inputs=[(s_act, w_done)])
                if tau == 1:
                    rec[2] = g
                evac_wait("vector", bk)
                xc = em.op("vector", lambda e, o_=Xc(o, tau), i=PS[bk][:, :],
                           sv=VB["s_w3"][:, o:o + 1]:
                           e.scalar_tensor_tensor(o_, i, sv, o_, ALU.mult, ALU.add),
                           inc=s_dve)
                bank_ev[bk] = (s_dve, xc)
            em.wait("sync", s_dve, xc)
            dsty = y_h[b, o * P:(o + 1) * P, hh * TT:(hh + 1) * TT]
            em.op("sync", lambda e, d=dsty, s=X[:, o * TT:(o + 1) * TT]:
                  e.dma_start(out=d, in_=s), inc=s_y, amount=16)

    em.wait("sync", s_y, 16 * DC * NPASS)

    with nc.Block() as block:
        @block.tensor
        def _(e):
            for fn in em.q["tensor"]:
                fn(e)

        @block.scalar
        def _(e):
            for fn in em.q["scalar"]:
                fn(e)

        @block.vector
        def _(e):
            for fn in em.q["vector"]:
                fn(e)

        @block.gpsimd
        def _(e):
            for fn in em.q["gpsimd"]:
                fn(e)

        @block.sync
        def _(e):
            for fn in em.q["sync"]:
                fn(e)

    es.close()
    return nc


def _prep_vec(v, c):
    v = np.ascontiguousarray(np.asarray(v, dtype=np.float32).reshape(c, P).T)
    return v


def kernel(**inputs):
    x = np.asarray(inputs["L0_attn_hidden_states"], dtype=np.float32)
    x_fm = np.ascontiguousarray(x.transpose(0, 2, 1))

    g = lambda n: np.asarray(inputs[n], dtype=np.float32)
    wT = lambda n: np.ascontiguousarray(g(n).T)

    common = {
        "wt_fc": wT("L0_attn_fc_w"), "wt_r": wT("L0_attn_fc_r_w"),
        "wt_i": wT("L0_attn_fc_i_w"), "wt_gate": wT("L0_attn_fc_gate_w"),
        "wt_mg": wT("L0_attn_mlp_gate_w"), "wt_mf": wT("L0_attn_mlp_fc_w"),
        "wt_out": wT("L0_attn_out_w"),
        "wt_w1": wT("L0_ffn_w1_w"), "wt_w2": wT("L0_ffn_w2_w"),
        "wt_h1": wT("L0_ffn_h1_w"), "wt_h2": wT("L0_ffn_h2_w"),
        "wt_w3": wT("L0_ffn_w3_w"),
        "b_fc": _prep_vec(g("L0_attn_fc_b"), DC),
        "b_r": _prep_vec(g("L0_attn_fc_r_b"), DC),
        "b_i": _prep_vec(g("L0_attn_fc_i_b"), DC),
        "b_gate": _prep_vec(g("L0_attn_fc_gate_b"), DC),
        "b_mg": _prep_vec(g("L0_attn_mlp_gate_b"), MC),
        "b_mf": _prep_vec(g("L0_attn_mlp_fc_b"), MC),
        "b_out": _prep_vec(g("L0_attn_out_b"), DC),
        "nw": _prep_vec(g("L0_attn_norm_w"), DC),
        "n2w": _prep_vec(g("L0_attn_norm2_w"), DC),
        "alv": _prep_vec(g("L0_attn_alpha").reshape(-1), DC),
        "s_w1": _prep_vec(g("L0_ffn_w1_s"), FC),
        "s_w2": _prep_vec(g("L0_ffn_w2_s"), FC),
        "s_h1": _prep_vec(g("L0_ffn_h1_s"), FC),
        "s_h2": _prep_vec(g("L0_ffn_h2_s"), FC),
        "s_w3": _prep_vec(g("L0_ffn_w3_s"), DC),
    }

    nc = build_program()
    in_maps = []
    for c in range(NCORES):
        m = dict(common)
        m["x"] = np.ascontiguousarray(x_fm[c * NB:(c + 1) * NB])
        in_maps.append(m)
    res = run_bass_kernel_spmd(nc, in_maps, core_ids=list(range(NCORES)))
    y_fm = np.concatenate([r["y"] for r in res.results], axis=0)
    return np.ascontiguousarray(y_fm.transpose(0, 2, 1))
